# version 05
# baseline (speedup 1.0000x reference)
import numpy as np
import concourse.bass as bass
import concourse.mybir as mybir
from concourse.bass_utils import run_bass_kernel_spmd
from contextlib import ExitStack

F32 = mybir.dt.float32
BF16 = mybir.dt.bfloat16
F8 = mybir.dt.float8e4
AF = mybir.ActivationFunctionType
ALU = mybir.AluOpType
AX = mybir.AxisListType.X

D_MODEL = 2048
N_CT = 16
INTER = 5632
N_IT = 44
KV = 4096
N_KT = 32
HEADS = 16
TOK_PER_CORE = 4096
TILE = 512
N_TILES = TOK_PER_CORE // TILE
SCALE = float(np.float32(1.0) / np.float32(np.sqrt(np.float32(128.0))))
LN224 = float(np.log(224.0))
EPS = 1e-5

ENGS = ("sync", "gpsimd", "vector", "scalar", "tensor")
import os
DEBUG = bool(int(os.environ.get("KDBG", "0")))


def build_program():
    nc = bass.Bass("TRN2", target_bir_lowering=False)
    dbg_outs = {}

    xt_in = nc.dram_tensor("xt_in", [D_MODEL, TOK_PER_CORE], F32, kind="ExternalInput").ap()
    wgT = nc.dram_tensor("wgT", [D_MODEL, INTER], F32, kind="ExternalInput").ap()
    wuT = nc.dram_tensor("wuT", [D_MODEL, INTER], F32, kind="ExternalInput").ap()
    wdT = nc.dram_tensor("wdT", [INTER, D_MODEL], F32, kind="ExternalInput").ap()
    wgN = nc.dram_tensor("wgN", [INTER, D_MODEL], F32, kind="ExternalInput").ap()
    wuN = nc.dram_tensor("wuN", [INTER, D_MODEL], F32, kind="ExternalInput").ap()
    wdN = nc.dram_tensor("wdN", [D_MODEL, INTER], F32, kind="ExternalInput").ap()
    wqT = nc.dram_tensor("wqT", [D_MODEL, D_MODEL], F32, kind="ExternalInput").ap()
    woT = nc.dram_tensor("woT", [D_MODEL, D_MODEL], F32, kind="ExternalInput").ap()
    ktF = nc.dram_tensor("ktF", [D_MODEL, KV], F32, kind="ExternalInput").ap()
    vF = nc.dram_tensor("vF", [KV, D_MODEL], F32, kind="ExternalInput").ap()
    ident_in = nc.dram_tensor("ident_in", [128, 128], F32, kind="ExternalInput").ap()
    out_t = nc.dram_tensor("out", [TOK_PER_CORE, D_MODEL], F32, kind="ExternalOutput").ap()

    qwg_s = nc.dram_tensor("qwg_s", [N_IT, 128, N_CT * 128], F8, kind="Internal").ap()
    qwu_s = nc.dram_tensor("qwu_s", [N_IT, 128, N_CT * 128], F8, kind="Internal").ap()
    qwd_s = nc.dram_tensor("qwd_s", [N_IT, 128, N_CT * 128], F8, kind="Internal").ap()
    kt_s = nc.dram_tensor("kt_s", [D_MODEL, KV], BF16, kind="Internal").ap()
    v_s = nc.dram_tensor("v_s", [HEADS, 128, N_KT * 128], BF16, kind="Internal").ap()
    wq_s = nc.dram_tensor("wq_s", [N_CT, 128, N_CT * 128], BF16, kind="Internal").ap()
    wo_s = nc.dram_tensor("wo_s", [N_CT, 128, N_CT * 128], BF16, kind="Internal").ap()

    with ExitStack() as ctx:
        sbt = lambda name, shape, dt: ctx.enter_context(nc.sbuf_tensor(name, shape, dt))

        X = sbt("X", [128, N_CT * TILE], F32)
        qx = sbt("qx", [128, N_CT * TILE], F8)
        arena = sbt("arena", [128, 24576], F32)
        wg_b = sbt("wg_b", [128, 2 * N_CT * 128], F8)
        wu_b = sbt("wu_b", [128, 2 * N_CT * 128], F8)
        wd_b = sbt("wd_b", [128, 2 * 2048], F8)
        fg_c = sbt("fg_c", [128, N_IT], F32)
        fu_c = sbt("fu_c", [128, N_IT], F32)
        fd_c = sbt("fd_c", [128, N_CT], F32)
        rc_g = sbt("rc_g", [128, N_IT], F32)
        rc_u = sbt("rc_u", [128, N_IT], F32)
        rc_d = sbt("rc_d", [128, N_CT], F32)
        ident = sbt("ident", [128, 128], F32)
        ones128 = sbt("ones128", [128, 128], F32)
        ones_bf = sbt("ones_bf", [128, 1], BF16)
        statc = sbt("statc", [128, 64], F32)
        rowsc = sbt("rowsc", [1, 1024], F32)
        b1 = sbt("b1", [128, TILE], F32)
        b2 = sbt("b2", [128, TILE], F32)
        b3 = sbt("b3", [128, TILE], F32)
        b4 = sbt("b4", [128, TILE], F32)
        b5 = sbt("b5", [128, TILE], F32)
        bctm = sbt("bctm", [128, 512], F32)
        cns = sbt("cns", [128, 2], F32)
        mxr = sbt("mxr", [128, TILE], F32)
        mnr = sbt("mnr", [128, TILE], F32)
        gv = sbt("gv", [128, 2 * TILE], F32)
        sg = sbt("sg", [128, 2 * TILE], F32)

        psum_all = ctx.enter_context(nc.psum_tensor("psum_all", [128, 4096], F32))
        PS = lambda bank: psum_all[:, bank * 512:(bank + 1) * 512]
        PSc = lambda bank, c0, c1, p0=0, p1=128: psum_all[p0:p1, bank * 512 + c0: bank * 512 + c1]

        SEM = {}
        for s in ("dA", "dB", "v", "s", "t"):
            SEM[s] = ctx.enter_context(nc.semaphore("sem_" + s))

        sched = {e: [] for e in ENGS}
        cnt = {"dA": 0, "dB": 0, "v": 0, "s": 0, "t": 0}
        esem = {"sync": "dA", "gpsimd": "dB", "vector": "v", "scalar": "s", "tensor": "t"}

        def emit(eng, fn, inc=False):
            sname = esem[eng]
            if inc:
                amt = 16 if sname in ("dA", "dB") else 1
                cnt[sname] += amt
                sched[eng].append(lambda e, fn=fn, amt=amt, sem=SEM[sname]: fn(e).then_inc(sem, amt))
            else:
                sched[eng].append(fn)
            return cnt[sname]

        def wait(eng, sname, val):
            if val <= 0:
                return
            sched[eng].append(lambda e, sem=SEM[sname], val=val: e.wait_ge(sem, val))

        def dma(eng, out, in_):
            return emit(eng, lambda e, o=out, i=in_: e.dma_start(out=o, in_=i), inc=True)

        def v_op(fn):
            return emit("vector", fn, inc=True)

        def v_ops(fn):
            r = emit("vector", fn, inc=True)
            wait("vector", "v", r)
            return r

        def s_op(fn):
            return emit("scalar", fn, inc=True)

        def t_op(fn):
            return emit("tensor", fn, inc=True)

        def t_quiet(fn):
            emit("tensor", fn, inc=False)

        def dump(name, ap, dt, shape, src_sem):
            if not DEBUG or name in dbg_outs:
                return
            t = nc.dram_tensor("dbg_" + name, shape, dt, kind="ExternalOutput").ap()
            dbg_outs[name] = True
            for sem in ("v", "s", "t"):
                wait("gpsimd", sem, cnt[sem])
            dv = dma("gpsimd", t, ap)
            for e in ("vector", "scalar", "tensor"):
                wait(e, "dB", dv)

        def bcast_from_cols(col_ap_fn, dst_sb, pbank):
            wait("vector", "v", cnt["v"])
            for q in range(4):
                v_op(lambda e, q=q, c=col_ap_fn(q): e.tensor_scalar(
                    out=bctm[:, q * 128:(q + 1) * 128], in0=ones128[:],
                    scalar1=c, scalar2=None, op0=ALU.mult))
            wait("tensor", "v", cnt["v"])
            for q in range(4):
                t_op(lambda e, q=q, pb=pbank: e.transpose(
                    out=PSc(pb, q * 128, (q + 1) * 128), in_=bctm[:, q * 128:(q + 1) * 128],
                    identity=ident[:]))
            wait("vector", "t", cnt["t"])
            v_op(lambda e, pb=pbank, d=dst_sb: e.tensor_copy(out=d, in_=PS(pb)))
            return cnt["v"]

        def part_reduce(src_ap, op, pbank, out_col):
            wait("tensor", "v", cnt["v"])
            for q in range(4):
                t_op(lambda e, q=q: e.transpose(
                    out=PSc(pbank, q * 128, q * 128 + 128),
                    in_=src_ap[:, q * 128:(q + 1) * 128], identity=ident[:]))
            wait("vector", "t", cnt["t"])
            for q in range(4):
                v_op(lambda e, q=q: e.tensor_reduce(
                    out=statc[:, out_col + q: out_col + q + 1],
                    in_=PSc(pbank, q * 128, q * 128 + 128), axis=AX, op=op))
            wait("vector", "v", cnt["v"])

        dma("gpsimd", ident[:], ident_in)
        v_op(lambda e: e.memset(ones128[:], 1.0))
        v_op(lambda e: e.memset(ones_bf[:], 1.0))
        v_op(lambda e: e.memset(cns[:, 0:1], LN224))
        v_op(lambda e: e.memset(cns[:, 1:2], EPS))
        wait("vector", "dB", cnt["dB"])

        AR_A, AR_B = 0, 8192
        AR_Q = 16384

        def scale_cols(nat_ap, ntiles, dst_cols, width):
            lastuse = [cnt["v"], cnt["v"]]
            for i in range(ntiles):
                slot = AR_A if i % 2 == 0 else AR_B
                wait("gpsimd", "v", lastuse[i % 2])
                dv = dma("gpsimd", arena[:, slot:slot + width], nat_ap[i * 128:(i + 1) * 128, :])
                wait("vector", "dB", dv)
                v_op(lambda e, i=i, slot=slot, dst=dst_cols, width=width: e.tensor_reduce(
                    out=dst[:, i:i + 1], in_=arena[:, slot:slot + width], axis=AX,
                    op=ALU.max, apply_absolute_value=True))
                lastuse[i % 2] = cnt["v"]

        scale_cols(wgN, N_IT, fg_c, 2048)
        scale_cols(wuN, N_IT, fu_c, 2048)
        scale_cols(wdN, N_CT, fd_c, 5632)
        wait("scalar", "v", cnt["v"])
        for src, rc, n in ((fg_c, rc_g, N_IT), (fu_c, rc_u, N_IT), (fd_c, rc_d, N_CT)):
            s_op(lambda e, src=src, n=n: e.activation(out=statc[:, 0:n], in_=src[:, 0:n], func=AF.Ln))
            s_op(lambda e, rc=rc, n=n: e.activation(out=rc[:, 0:n], in_=statc[:, 0:n], func=AF.Exp, scale=-1.0, bias=cns[:, 0:1]))
        wait("vector", "s", cnt["s"])
        wait("vector", "v", cnt["v"])
        v_ops(lambda e: e.tensor_scalar(out=fg_c[:], in0=fg_c[:], scalar1=1.0 / 224.0, scalar2=None, op0=ALU.mult))
        v_ops(lambda e: e.tensor_scalar(out=fu_c[:], in0=fu_c[:], scalar1=1.0 / 224.0, scalar2=None, op0=ALU.mult))
        v_ops(lambda e: e.tensor_scalar(out=fd_c[:], in0=fd_c[:], scalar1=0.5 / 224.0, scalar2=None, op0=ALU.mult))

        def quant_T(srcT, rc, out_scr):
            lastuse = [cnt["v"], cnt["v"]]
            prev_store = cnt["dB"]
            for ic in range(11):
                slot = AR_A if ic % 2 == 0 else AR_B
                wait("sync", "v", lastuse[ic % 2])
                dvl = None
                for k in range(N_CT):
                    dvl = dma("sync", arena[:, slot + k * 512: slot + (k + 1) * 512],
                              srcT[k * 128:(k + 1) * 128, ic * 512:(ic + 1) * 512])
                for q in range(4):
                    v_op(lambda e, q=q, rc=rc, ic=ic: e.tensor_scalar(
                        out=bctm[:, q * 128:(q + 1) * 128], in0=ones128[:],
                        scalar1=rc[:, ic * 4 + q: ic * 4 + q + 1], scalar2=None, op0=ALU.mult))
                wait("tensor", "v", cnt["v"])
                for q in range(4):
                    t_op(lambda e, q=q: e.transpose(
                        out=PSc(6, q * 128, (q + 1) * 128), in_=bctm[:, q * 128:(q + 1) * 128],
                        identity=ident[:]))
                wait("vector", "t", cnt["t"])
                wait("vector", "dA", dvl)
                wait("vector", "dB", prev_store)
                for k in range(N_CT):
                    v_op(lambda e, k=k, slot=slot: e.tensor_tensor(
                        out=arena[:, AR_Q + k * 128: AR_Q + k * 128 + 128].bitcast(F8),
                        in0=arena[:, slot + k * 512: slot + (k + 1) * 512],
                        in1=PS(6), op=ALU.mult))
                lastuse[ic % 2] = cnt["v"]
                wait("gpsimd", "v", cnt["v"])
                src3 = arena[:, AR_Q:AR_Q + 2048].bitcast(F8).rearrange("p (k i q) -> p k i q", k=N_CT, i=4)
                for i4 in range(4):
                    it = ic * 4 + i4
                    dma("gpsimd", out_scr[it].rearrange("p (k q) -> p k q", k=N_CT),
                        src3[:, :, i4, :])
                prev_store = cnt["dB"]
            wait("vector", "dB", cnt["dB"])

        quant_T(wgT, rc_g, qwg_s)
        quant_T(wuT, rc_u, qwu_s)

        for q in range(16):
            v_op(lambda e, q=q: e.tensor_scalar(
                out=bctm[:, 0:128], in0=ones128[:],
                scalar1=rc_d[:, q:q + 1], scalar2=None, op0=ALU.mult))
            wait("tensor", "v", cnt["v"])
            t_op(lambda e, q=q: e.transpose(
                out=PSc(4 + q // 4, (q % 4) * 128, (q % 4) * 128 + 128),
                in_=bctm[:, 0:128], identity=ident[:]))
            wait("vector", "t", cnt["t"])
        lastuse = [cnt["v"], cnt["v"]]
        prev_store = cnt["dB"]
        for rt in range(N_IT):
            slot = AR_A if rt % 2 == 0 else AR_B
            wait("sync", "v", lastuse[rt % 2])
            dv = dma("sync", arena[:, slot:slot + 2048], wdT[rt * 128:(rt + 1) * 128, :])
            wait("vector", "dA", dv)
            wait("vector", "dB", prev_store)
            for cc in range(4):
                v_op(lambda e, cc=cc, slot=slot: e.tensor_tensor(
                    out=arena[:, AR_Q + cc * 128: AR_Q + cc * 128 + 128].bitcast(F8),
                    in0=arena[:, slot + cc * 512: slot + (cc + 1) * 512],
                    in1=PS(4 + cc), op=ALU.mult))
            lastuse[rt % 2] = cnt["v"]
            wait("gpsimd", "v", cnt["v"])
            dma("gpsimd", qwd_s[rt], arena[:, AR_Q:AR_Q + 512].bitcast(F8))
            prev_store = cnt["dB"]
        wait("scalar", "dB", cnt["dB"])

        wait("sync", "v", cnt["v"])
        lastuse = [cnt["s"], cnt["s"]]
        for i in range(N_CT):
            slot = AR_A if i % 2 == 0 else AR_B
            wait("sync", "s", lastuse[i % 2])
            dv = dma("sync", arena[:, slot:slot + 4096], ktF[i * 128:(i + 1) * 128, :])
            wait("scalar", "dA", dv)
            wait("scalar", "dB", cnt["dB"])
            sv = s_op(lambda e, slot=slot: e.copy(out=arena[:, AR_Q:AR_Q + 2048].bitcast(BF16), in_=arena[:, slot:slot + 4096]))
            lastuse[i % 2] = sv
            wait("gpsimd", "s", sv)
            dma("gpsimd", kt_s[i * 128:(i + 1) * 128, :], arena[:, AR_Q:AR_Q + 2048].bitcast(BF16))
        for k in range(N_KT):
            slot = AR_A if k % 2 == 0 else AR_B
            wait("sync", "s", lastuse[k % 2])
            dv = dma("sync", arena[:, slot:slot + 2048], vF[k * 128:(k + 1) * 128, :])
            wait("scalar", "dA", dv)
            wait("scalar", "dB", cnt["dB"])
            sv = s_op(lambda e, slot=slot: e.copy(out=arena[:, AR_Q:AR_Q + 1024].bitcast(BF16), in_=arena[:, slot:slot + 2048]))
            lastuse[k % 2] = sv
            wait("gpsimd", "s", sv)
            for h in range(HEADS):
                dma("gpsimd", v_s[h][:, k * 128:(k + 1) * 128],
                    arena[:, AR_Q:AR_Q + 1024].bitcast(BF16)[:, h * 128:(h + 1) * 128])
        for (srcW, dstW) in ((wqT, wq_s), (woT, wo_s)):
            for k in range(N_CT):
                slot = AR_A if k % 2 == 0 else AR_B
                wait("sync", "s", lastuse[k % 2])
                dv = dma("sync", arena[:, slot:slot + 2048], srcW[k * 128:(k + 1) * 128, :])
                wait("scalar", "dA", dv)
                wait("scalar", "dB", cnt["dB"])
                sv = s_op(lambda e, slot=slot: e.copy(out=arena[:, AR_Q:AR_Q + 1024].bitcast(BF16), in_=arena[:, slot:slot + 2048]))
                lastuse[k % 2] = sv
                wait("gpsimd", "s", sv)
                for j in range(N_CT):
                    dma("gpsimd", dstW[j][:, k * 128:(k + 1) * 128],
                        arena[:, AR_Q:AR_Q + 1024].bitcast(BF16)[:, j * 128:(j + 1) * 128])

        if DEBUG:
            dump("fg_c", fg_c[:], F32, [128, N_IT], "v")
            dump("rc_g", rc_g[:], F32, [128, N_IT], "s")
            dump("cns", cns[:], F32, [128, 2], "v")
            dump("fd_c", fd_c[:], F32, [128, N_CT], "v")
        snap = dict(cnt)
        for e in ENGS:
            for s in ("v", "s", "t", "dA", "dB"):
                if esem[e] != s:
                    wait(e, s, snap[s])

        T_OFF = 0
        XSQ = 0
        ACC = 8192
        A_XH = 0
        A_QT = 4096
        A_AO = 8192
        A_KT = 12288
        A_V = 16384
        A_WQ = 20480
        A_P = 22528
        OUTC = 12288

        C_S1, C_S2, C_MX, C_MN = 0, 4, 8, 12
        C_M, C_VAR, C_ISTD, C_RMX, C_R1, C_R2, C_FT, C_T1, C_T2 = 16, 20, 24, 28, 32, 36, 40, 44, 48
        C_RT, C_B5 = 52, 56

        def sum_tree(src_fn):
            if src_fn is None:
                a0, a1 = X[:, 0:4096], X[:, 4096:8192]
            else:
                a0, a1 = arena[:, src_fn:src_fn + 4096], arena[:, src_fn + 4096:src_fn + 8192]
            v_op(lambda e, a0=a0, a1=a1: e.tensor_tensor(out=arena[:, ACC:ACC + 4096], in0=a0, in1=a1, op=ALU.add))
            for w in (2048, 1024, 512):
                v_op(lambda e, w=w: e.tensor_tensor(out=arena[:, ACC:ACC + w], in0=arena[:, ACC:ACC + w], in1=arena[:, ACC + w:ACC + 2 * w], op=ALU.add))

        def ln_stats(want_max):
            wait("scalar", "v", cnt["v"])
            sum_tree(None)
            part_reduce(arena[:, ACC:ACC + 512], ALU.add, 0, C_S1)
            sq = s_op(lambda e: e.activation(out=arena[:, XSQ:XSQ + 8192], in_=X[:], func=AF.Square))
            wait("vector", "s", sq)
            sum_tree(XSQ)
            part_reduce(arena[:, ACC:ACC + 512], ALU.add, 1, C_S2)
            if want_max:
                for op, bank, col in ((ALU.max, 2, C_MX), (ALU.min, 3, C_MN)):
                    v_op(lambda e, op=op: e.tensor_tensor(out=arena[:, ACC:ACC + 4096], in0=X[:, 0:4096], in1=X[:, 4096:8192], op=op))
                    for w in (2048, 1024, 512):
                        v_op(lambda e, w=w, op=op: e.tensor_tensor(out=arena[:, ACC:ACC + w], in0=arena[:, ACC:ACC + w], in1=arena[:, ACC + w:ACC + 2 * w], op=op))
                    part_reduce(arena[:, ACC:ACC + 512], op, bank, col)
            v_ops(lambda e: e.tensor_scalar(out=statc[:, C_M:C_M + 4], in0=statc[:, C_S1:C_S1 + 4], scalar1=1.0 / 2048.0, scalar2=None, op0=ALU.mult))
            v_ops(lambda e: e.tensor_tensor(out=statc[:, C_VAR:C_VAR + 4], in0=statc[:, C_M:C_M + 4], in1=statc[:, C_M:C_M + 4], op=ALU.mult))
            v_ops(lambda e: e.scalar_tensor_tensor(out=statc[:, C_VAR:C_VAR + 4], in0=statc[:, C_S2:C_S2 + 4], scalar=1.0 / 2048.0, in1=statc[:, C_VAR:C_VAR + 4], op0=ALU.mult, op1=ALU.subtract))
            wait("scalar", "v", cnt["v"])
            s_op(lambda e: e.activation(out=statc[:, C_T1:C_T1 + 4], in_=statc[:, C_VAR:C_VAR + 4], func=AF.Ln, bias=cns[:, 1:2]))
            s_op(lambda e: e.activation(out=statc[:, C_ISTD:C_ISTD + 4], in_=statc[:, C_T1:C_T1 + 4], func=AF.Exp, scale=-0.5))
            wait("vector", "s", cnt["s"])
            if want_max:
                v_ops(lambda e: e.tensor_tensor(out=statc[:, C_T1:C_T1 + 4], in0=statc[:, C_MX:C_MX + 4], in1=statc[:, C_M:C_M + 4], op=ALU.subtract))
                v_ops(lambda e: e.tensor_tensor(out=statc[:, C_T2:C_T2 + 4], in0=statc[:, C_M:C_M + 4], in1=statc[:, C_MN:C_MN + 4], op=ALU.subtract))
                v_ops(lambda e: e.tensor_tensor(out=statc[:, C_RMX:C_RMX + 4], in0=statc[:, C_T1:C_T1 + 4], in1=statc[:, C_T2:C_T2 + 4], op=ALU.max))
                wait("scalar", "v", cnt["v"])
                s_op(lambda e: e.activation(out=statc[:, C_T1:C_T1 + 4], in_=statc[:, C_RMX:C_RMX + 4], func=AF.Ln))
                s_op(lambda e: e.activation(out=statc[:, C_R1:C_R1 + 4], in_=statc[:, C_T1:C_T1 + 4], func=AF.Exp, scale=-1.0, bias=cns[:, 0:1]))
                wait("vector", "s", cnt["s"])
                v_ops(lambda e: e.tensor_tensor(out=statc[:, C_R2:C_R2 + 4], in0=statc[:, C_M:C_M + 4], in1=statc[:, C_R1:C_R1 + 4], op=ALU.mult))
                v_ops(lambda e: e.tensor_tensor(out=statc[:, C_FT:C_FT + 4], in0=statc[:, C_ISTD:C_ISTD + 4], in1=statc[:, C_RMX:C_RMX + 4], op=ALU.mult))
                v_ops(lambda e: e.tensor_scalar(out=statc[:, C_FT:C_FT + 4], in0=statc[:, C_FT:C_FT + 4], scalar1=1.0 / 224.0, scalar2=None, op0=ALU.mult))

        last_wguse = [0, 0]
        last_wduse = [0, 0]
        last_kvuse = [0, 0]
        last_wquse = [0, 0]
        bankev = {}

        def ffn_pass():
            ln_stats(want_max=True)
            dump("statc", statc[:], F32, [128, 64], "v")
            bcast_from_cols(lambda q: statc[:, C_R1 + q:C_R1 + q + 1], b1[:], 6)
            bcast_from_cols(lambda q: statc[:, C_R2 + q:C_R2 + q + 1], b2[:], 7)
            bcast_from_cols(lambda q: statc[:, C_FT + q:C_FT + q + 1], b3[:], 6)
            for k in range(N_CT):
                v_op(lambda e, k=k: e.tensor_tensor(out=bctm[:], in0=X[:, k * 512:(k + 1) * 512], in1=b1[:], op=ALU.mult))
                v_op(lambda e, k=k: e.tensor_tensor(out=qx[:, k * 512:(k + 1) * 512], in0=bctm[:], in1=b2[:], op=ALU.subtract))
            qx_ready = cnt["v"]
            dump("b1", b1[:], F32, [128, 512], "v")
            dump("b2", b2[:], F32, [128, 512], "v")
            dump("b3", b3[:], F32, [128, 512], "v")
            dump("qx", qx[:], F8, [128, N_CT * TILE], "v")

            for i in range(N_IT):
                sl = i % 2
                wait("sync", "t", last_wguse[sl])
                dma("sync", wg_b[:, sl * 2048:sl * 2048 + 2048], qwg_s[i])
                du = dma("sync", wu_b[:, sl * 2048:sl * 2048 + 2048], qwu_s[i])
                wait("tensor", "dA", du)
                if i == 0:
                    wait("tensor", "v", qx_ready)
                wait("tensor", "v", bankev.get(("g", sl), 0))
                for k in range(N_CT):
                    t_quiet(lambda e, sl=sl, k=k: e.matmul(
                        out=PS(0 + sl), lhsT=wg_b[:, sl * 2048 + k * 128:sl * 2048 + (k + 1) * 128],
                        rhs=qx[:, k * 512:(k + 1) * 512], start=(k == 0), stop=(k == N_CT - 1)))
                wait("tensor", "v", bankev.get(("u", sl), 0))
                for k in range(N_CT - 1):
                    t_quiet(lambda e, sl=sl, k=k: e.matmul(
                        out=PS(2 + sl), lhsT=wu_b[:, sl * 2048 + k * 128:sl * 2048 + (k + 1) * 128],
                        rhs=qx[:, k * 512:(k + 1) * 512], start=(k == 0), stop=False))
                gu_done = t_op(lambda e, sl=sl, k=N_CT - 1: e.matmul(
                    out=PS(2 + sl), lhsT=wu_b[:, sl * 2048 + k * 128:sl * 2048 + (k + 1) * 128],
                    rhs=qx[:, k * 512:(k + 1) * 512], start=False, stop=True))
                last_wguse[sl] = gu_done

                wait("vector", "t", gu_done)
                v_op(lambda e, sl=sl, i=i: e.scalar_tensor_tensor(
                    out=gv[:, sl * 512:sl * 512 + 512], in0=PS(0 + sl),
                    scalar=fg_c[:, i:i + 1], in1=b3[:], op0=ALU.mult, op1=ALU.mult))
                bankev[("g", sl)] = cnt["v"]
                wait("scalar", "v", cnt["v"])
                sgs = s_op(lambda e, sl=sl: e.activation(
                    out=sg[:, sl * 512:sl * 512 + 512], in_=gv[:, sl * 512:sl * 512 + 512], func=AF.Silu))
                wait("vector", "s", sgs)
                v_op(lambda e, sl=sl, i=i: e.scalar_tensor_tensor(
                    out=arena[:, T_OFF + i * 512:T_OFF + (i + 1) * 512], in0=PS(2 + sl),
                    scalar=fu_c[:, i:i + 1], in1=sg[:, sl * 512:sl * 512 + 512],
                    op0=ALU.mult, op1=ALU.mult))
                bankev[("u", sl)] = cnt["v"]
                if i == 0:
                    dump("wg0", wg_b[:, 0:2048], F8, [128, 2048], "dA")
                    dump("gv0", gv[:, 0:512], F32, [128, 512], "v")
                    dump("t0", arena[:, T_OFF:T_OFF + 512], F32, [128, 512], "v")
                if i == 0:
                    v_op(lambda e: e.tensor_copy(out=mxr[:], in_=arena[:, T_OFF:T_OFF + 512]))
                    v_op(lambda e: e.tensor_copy(out=mnr[:], in_=arena[:, T_OFF:T_OFF + 512]))
                else:
                    v_op(lambda e, i=i: e.tensor_tensor(out=mxr[:], in0=mxr[:], in1=arena[:, T_OFF + i * 512:T_OFF + (i + 1) * 512], op=ALU.max))
                    v_op(lambda e, i=i: e.tensor_tensor(out=mnr[:], in0=mnr[:], in1=arena[:, T_OFF + i * 512:T_OFF + (i + 1) * 512], op=ALU.min))

            part_reduce(mxr[:], ALU.max, 6, C_T1)
            part_reduce(mnr[:], ALU.min, 7, C_T2)
            v_ops(lambda e: e.tensor_scalar(out=statc[:, C_T2:C_T2 + 4], in0=statc[:, C_T2:C_T2 + 4], scalar1=-1.0, scalar2=None, op0=ALU.mult))
            v_ops(lambda e: e.tensor_tensor(out=statc[:, C_RT:C_RT + 4], in0=statc[:, C_T1:C_T1 + 4], in1=statc[:, C_T2:C_T2 + 4], op=ALU.max))
            wait("scalar", "v", cnt["v"])
            s_op(lambda e: e.activation(out=statc[:, C_T1:C_T1 + 4], in_=statc[:, C_RT:C_RT + 4], func=AF.Ln))
            s_op(lambda e: e.activation(out=statc[:, C_T2:C_T2 + 4], in_=statc[:, C_T1:C_T1 + 4], func=AF.Exp, scale=-1.0, bias=cns[:, 0:1]))
            wait("vector", "s", cnt["s"])
            v_ops(lambda e: e.tensor_tensor(out=statc[:, C_B5:C_B5 + 4], in0=statc[:, C_RT:C_RT + 4], in1=statc[:, C_FT:C_FT + 4], op=ALU.mult))
            v_ops(lambda e: e.tensor_scalar(out=statc[:, C_B5:C_B5 + 4], in0=statc[:, C_B5:C_B5 + 4], scalar1=1.0 / 224.0, scalar2=None, op0=ALU.mult))
            bcast_from_cols(lambda q: statc[:, C_T2 + q:C_T2 + q + 1], b4[:], 6)
            bcast_from_cols(lambda q: statc[:, C_B5 + q:C_B5 + q + 1], b5[:], 7)
            dump("mxr", mxr[:], F32, [128, 512], "v")
            dump("b4q", b4[:], F32, [128, 512], "v")
            for i in range(N_IT):
                v_op(lambda e, i=i: e.tensor_tensor(
                    out=arena[:, T_OFF + i * 512:T_OFF + i * 512 + 128].bitcast(F8),
                    in0=arena[:, T_OFF + i * 512:T_OFF + (i + 1) * 512], in1=b4[:], op=ALU.mult))
            qt_ready = cnt["v"]
            dump("qt0", arena[:, T_OFF:T_OFF + 128].bitcast(F8), F8, [128, 512], "v")

            dump("b5d", b5[:], F32, [128, 512], "v")
            dump("qwd0", qwd_s[0], F8, [128, 2048], "v")
            dump("qt5", arena[:, T_OFF + 5 * 512:T_OFF + 5 * 512 + 128].bitcast(F8), F8, [128, 512], "v")
            for half in range(2):
                for rt in range(N_IT):
                    sl = rt % 2
                    wait("sync", "t", last_wduse[sl])
                    dd = dma("sync", wd_b[:, sl * 2048:sl * 2048 + 2048], qwd_s[rt])
                    wait("tensor", "dA", dd)
                    if half == 0 and rt == 0:
                        wait("tensor", "v", qt_ready)
                    if rt == 0:
                        for jb in range(8):
                            wait("tensor", "v", bankev.get(("dn", jb), 0))
                    for jb in range(8):
                        j = half * 8 + jb
                        mk = lambda e, sl=sl, j=j, jb=jb, rt=rt: e.matmul(
                            out=PS(jb), lhsT=wd_b[:, sl * 2048 + j * 128:sl * 2048 + (j + 1) * 128],
                            rhs=arena[:, T_OFF + rt * 512:T_OFF + rt * 512 + 128].bitcast(F8),
                            start=(rt == 0), stop=(rt == N_IT - 1))
                        if jb == 7:
                            last_wduse[sl] = t_op(mk)
                        else:
                            t_quiet(mk)
                dn_done = cnt["t"]
                wait("vector", "t", dn_done)
                for jb in range(8):
                    j = half * 8 + jb
                    v_op(lambda e, jb=jb, j=j: e.scalar_tensor_tensor(
                        out=gv[:, (jb % 2) * 512:(jb % 2) * 512 + 512], in0=PS(jb),
                        scalar=fd_c[:, j:j + 1], in1=b5[:], op0=ALU.mult, op1=ALU.mult))
                    if half == 1 and jb == 3:
                        dump("gv11", gv[:, 512:1024], F32, [128, 512], "v")
                    v_op(lambda e, jb=jb, j=j: e.tensor_tensor(
                        out=X[:, j * 512:(j + 1) * 512], in0=X[:, j * 512:(j + 1) * 512],
                        in1=gv[:, (jb % 2) * 512:(jb % 2) * 512 + 512], op=ALU.add))
                    bankev[("dn", jb)] = cnt["v"]

        def attn_pass():
            ln_stats(want_max=False)
            v_ops(lambda e: e.tensor_tensor(out=statc[:, C_T1:C_T1 + 4], in0=statc[:, C_M:C_M + 4], in1=statc[:, C_ISTD:C_ISTD + 4], op=ALU.mult))
            bcast_from_cols(lambda q: statc[:, C_ISTD + q:C_ISTD + q + 1], b1[:], 6)
            bcast_from_cols(lambda q: statc[:, C_T1 + q:C_T1 + q + 1], b2[:], 7)
            xh = lambda k: arena[:, A_XH + k * 256:A_XH + (k + 1) * 256].bitcast(BF16)
            for k in range(N_CT):
                v_op(lambda e, k=k: e.tensor_tensor(out=bctm[:], in0=X[:, k * 512:(k + 1) * 512], in1=b1[:], op=ALU.mult))
                v_op(lambda e, k=k: e.tensor_tensor(out=xh(k), in0=bctm[:], in1=b2[:], op=ALU.subtract))
            xh_ready = cnt["v"]

            qtile = lambda j: arena[:, A_QT + j * 256:A_QT + (j + 1) * 256].bitcast(BF16)
            wqb = lambda sl: arena[:, A_WQ + sl * 1024:A_WQ + sl * 1024 + 1024].bitcast(BF16)
            for j in range(N_CT):
                sl = j % 2
                if j < 2:
                    wait("sync", "t", cnt["t"])
                wait("sync", "s", last_wquse[sl])
                dq = dma("sync", wqb(sl), wq_s[j])
                wait("tensor", "dA", dq)
                if j == 0:
                    wait("tensor", "v", xh_ready)
                    wait("tensor", "s", bankev.get(("q", 0), 0))
                wait("tensor", "s", bankev.get(("q", sl), 0))
                for k in range(N_CT - 1):
                    t_quiet(lambda e, sl=sl, k=k: e.matmul(
                        out=PS(0 + sl), lhsT=wqb(sl)[:, k * 128:(k + 1) * 128],
                        rhs=xh(k), start=(k == 0), stop=False))
                qj = t_op(lambda e, sl=sl, k=N_CT - 1: e.matmul(
                    out=PS(0 + sl), lhsT=wqb(sl)[:, k * 128:(k + 1) * 128],
                    rhs=xh(k), start=False, stop=True))
                wait("scalar", "t", qj)
                sv = s_op(lambda e, sl=sl, j=j: e.activation(out=qtile(j), in_=PS(0 + sl), func=AF.Copy, scale=SCALE))
                bankev[("q", sl)] = sv
                last_wquse[sl] = sv
            qt_all = cnt["s"]
            dump("qtile0", arena[:, A_QT:A_QT + 256].bitcast(BF16), BF16, [128, 512], "s")

            pslot = lambda k: arena[:, A_P + (k % 8) * 256:A_P + (k % 8) * 256 + 256].bitcast(BF16)
            ktb = lambda sl: arena[:, A_KT + sl * 2048:A_KT + (sl + 1) * 2048].bitcast(BF16)
            vbuf = lambda sl: arena[:, A_V + sl * 2048:A_V + (sl + 1) * 2048].bitcast(BF16)
            last_exp = {}
            for h in range(HEADS):
                sl = h % 2
                if h < 2:
                    wait("sync", "t", cnt["t"])
                wait("sync", "t", last_kvuse[sl])
                if h == 0:
                    wait("sync", "dB", cnt["dB"])
                dma("sync", ktb(sl), kt_s[h * 128:(h + 1) * 128, :])
                dkv = dma("sync", vbuf(sl), v_s[h])
                wait("tensor", "dA", dkv)
                if h == 0:
                    wait("tensor", "s", qt_all)
                for k in range(N_KT):
                    wait("tensor", "s", last_exp.get(k % 2, 0))
                    sk = t_op(lambda e, sl=sl, k=k, h=h: e.matmul(
                        out=PS(2 + k % 2), lhsT=ktb(sl)[:, k * 128:(k + 1) * 128],
                        rhs=qtile(h), start=True, stop=True))
                    wait("scalar", "t", sk)
                    pk = s_op(lambda e, k=k: e.activation(out=pslot(k), in_=PS(2 + k % 2), func=AF.Exp))
                    last_exp[k % 2] = pk
                    wait("tensor", "s", pk)
                    t_quiet(lambda e, sl=sl, k=k: e.matmul(
                        out=PS(4), lhsT=vbuf(sl)[:, k * 128:(k + 1) * 128], rhs=pslot(k),
                        start=(k == 0), stop=(k == N_KT - 1)))
                    dmk = lambda e, k=k: e.matmul(
                        out=PSc(5, 0, 512, 0, 1), lhsT=ones_bf[:], rhs=pslot(k),
                        start=(k == 0), stop=(k == N_KT - 1))
                    if k == N_KT - 1:
                        pv_done = t_op(dmk)
                    else:
                        t_quiet(dmk)
                last_kvuse[sl] = pv_done
                wait("scalar", "t", pv_done)
                s_op(lambda e: e.activation(out=rowsc[0:1, 0:512], in_=PSc(5, 0, 512, 0, 1), func=AF.Ln))
                dr = s_op(lambda e: e.activation(out=rowsc[0:1, 512:1024], in_=rowsc[0:1, 0:512], func=AF.Exp, scale=-1.0))
                wait("tensor", "s", dr)
                bc = t_op(lambda e: e.matmul(out=PS(6), lhsT=ones128[0:1, :], rhs=rowsc[0:1, 512:1024], start=True, stop=True))
                wait("vector", "t", bc)
                v_op(lambda e: e.tensor_copy(out=b4[:], in_=PS(6)))
                v_op(lambda e, h=h: e.tensor_tensor(
                    out=arena[:, A_AO + h * 256:A_AO + (h + 1) * 256].bitcast(BF16),
                    in0=PS(4), in1=b4[:], op=ALU.mult))
                if h == 0:
                    dump("p31", arena[:, A_P + 7 * 256:A_P + 7 * 256 + 256].bitcast(BF16), BF16, [128, 512], "s")
                    dump("denb", b4[:], F32, [128, 512], "v")
                    dump("ao0", arena[:, A_AO:A_AO + 256].bitcast(BF16), BF16, [128, 512], "v")
                wait("tensor", "v", cnt["v"])

            ao_ready = cnt["v"]
            for j in range(N_CT):
                sl = j % 2
                wait("sync", "v", bankev.get(("wo", sl), 0))
                do = dma("sync", wqb(sl), wo_s[j])
                wait("tensor", "dA", do)
                if j == 0:
                    wait("tensor", "v", ao_ready)
                wait("tensor", "v", bankev.get(("po", sl), 0))
                for k in range(N_CT - 1):
                    t_quiet(lambda e, sl=sl, k=k: e.matmul(
                        out=PS(0 + sl), lhsT=wqb(sl)[:, k * 128:(k + 1) * 128],
                        rhs=arena[:, A_AO + k * 256:A_AO + (k + 1) * 256].bitcast(BF16),
                        start=(k == 0), stop=False))
                oj = t_op(lambda e, sl=sl, k=N_CT - 1: e.matmul(
                    out=PS(0 + sl), lhsT=wqb(sl)[:, k * 128:(k + 1) * 128],
                    rhs=arena[:, A_AO + k * 256:A_AO + (k + 1) * 256].bitcast(BF16),
                    start=False, stop=True))
                wait("vector", "t", oj)
                v_op(lambda e, sl=sl, j=j: e.tensor_tensor(
                    out=X[:, j * 512:(j + 1) * 512], in0=X[:, j * 512:(j + 1) * 512], in1=PS(0 + sl), op=ALU.add))
                bankev[("po", sl)] = cnt["v"]
                bankev[("wo", sl)] = cnt["v"]

        for tile in range(N_TILES):
            wait("gpsimd", "t", cnt["t"])
            dx = None
            for ct in range(N_CT):
                dx = dma("gpsimd", X[:, ct * 512:(ct + 1) * 512],
                         xt_in[ct * 128:(ct + 1) * 128, tile * 512:(tile + 1) * 512])
            for e in ("vector", "scalar", "tensor"):
                wait(e, "dB", dx)

            ffn_pass()
            dump("x1", X[:], F32, [128, N_CT * TILE], "v")
            attn_pass()
            dump("x2", X[:], F32, [128, N_CT * TILE], "v")
            ffn_pass()

            wait("tensor", "v", cnt["v"])
            for g in range(4):
                for ct in range(N_CT):
                    t_op(lambda e, g=g, ct=ct: e.transpose(
                        out=psum_all[:, ct * 128:(ct + 1) * 128],
                        in_=X[:, ct * 512 + g * 128: ct * 512 + (g + 1) * 128],
                        identity=ident[:]))
                tg = cnt["t"]
                wait("scalar", "t", tg)
                wait("scalar", "dB", cnt["dB"])
                sv = s_op(lambda e: e.copy(out=arena[:, OUTC:OUTC + 2048], in_=psum_all[:, 0:2048]))
                wait("gpsimd", "s", sv)
                dma("gpsimd", out_t[tile * 512 + g * 128: tile * 512 + (g + 1) * 128, :],
                    arena[:, OUTC:OUTC + 2048])
                wait("tensor", "s", sv)

        sched["gpsimd"].append(lambda e, v=cnt["dB"]: e.wait_ge(SEM["dB"], v))

        block = ctx.enter_context(nc.Block())

        @block.sync
        def _(sync):
            for fn in sched["sync"]:
                fn(sync)

        @block.gpsimd
        def _(gpsimd):
            for fn in sched["gpsimd"]:
                fn(gpsimd)

        @block.vector
        def _(vector):
            for fn in sched["vector"]:
                fn(vector)

        @block.scalar
        def _(scalar):
            for fn in sched["scalar"]:
                fn(scalar)

        @block.tensor
        def _(tensor):
            for fn in sched["tensor"]:
                fn(tensor)

    return nc


def kernel(L0_attn_query, L0_attn_kv_k, L0_attn_kv_v, L0_attn_Wq, L0_attn_Wo,
           L0_ffn_w_gate, L0_ffn_w_up, L0_ffn_w_down,
           L0_ffn_up_reorder_index, L0_ffn_down_reorder_index):
    x = np.ascontiguousarray(np.asarray(L0_attn_query, dtype=np.float32)).reshape(-1, D_MODEL)
    D = np.asarray(L0_ffn_down_reorder_index).astype(np.int64)
    Wg = np.asarray(L0_ffn_w_gate, dtype=np.float32)
    Wu = np.asarray(L0_ffn_w_up, dtype=np.float32)
    Wd = np.asarray(L0_ffn_w_down, dtype=np.float32)
    Wq = np.asarray(L0_attn_Wq, dtype=np.float32)
    Wo = np.asarray(L0_attn_Wo, dtype=np.float32)
    kvk = np.asarray(L0_attn_kv_k, dtype=np.float32).reshape(-1, D_MODEL)
    kvv = np.asarray(L0_attn_kv_v, dtype=np.float32).reshape(-1, D_MODEL)

    WgP = Wg[D]
    WuP = Wu[D]
    shared = {
        "wgT": np.ascontiguousarray(WgP.T),
        "wuT": np.ascontiguousarray(WuP.T),
        "wdT": np.ascontiguousarray(Wd.T),
        "wgN": np.ascontiguousarray(WgP),
        "wuN": np.ascontiguousarray(WuP),
        "wdN": np.ascontiguousarray(Wd),
        "wqT": np.ascontiguousarray(Wq.T),
        "woT": np.ascontiguousarray(Wo.T),
        "ktF": np.ascontiguousarray(kvk.T),
        "vF": np.ascontiguousarray(kvv),
        "ident_in": np.eye(128, dtype=np.float32),
    }
    in_maps = []
    for c in range(8):
        m = dict(shared)
        m["xt_in"] = np.ascontiguousarray(x[c * TOK_PER_CORE:(c + 1) * TOK_PER_CORE].T)
        in_maps.append(m)

    nc = build_program()
    res = run_bass_kernel_spmd(nc, in_maps, core_ids=list(range(8)))
    out = np.concatenate([res.results[c]["out"] for c in range(8)], axis=0)
    return out.reshape(4, 8192, D_MODEL).astype(np.float32)
